# version 13
# baseline (speedup 1.0000x reference)
import numpy as np
import ml_dtypes
from contextlib import ExitStack

import concourse.bass as bass
import concourse.mybir as mybir
from concourse.bass_utils import run_bass_kernel_spmd

F32 = mybir.dt.float32
BF16 = mybir.dt.bfloat16
AF = mybir.ActivationFunctionType
ALU = mybir.AluOpType

D = 2048
T = 2048
NH = 16
OC = 64
DC = 16
ICN = 16
NTQ = 4
QT = 16
NOG = 4
SQS = 0.08838834764831845
WHS = 0.011048543456039806
NEGBIG = -1.0e30
EPS = 1.0e-5

WRN = 7
FRN = 3
SGFN = 4
SGBN = 7
XBN = 3
S1N = 2
PTN = 3
RBN = 1


def _hadamard(n):
    m = np.array([[1.0]], dtype=np.float32)
    while m.shape[0] < n:
        m = np.block([[m, m], [m, -m]])
    return m


class Prog:
    ENGS = "TVSYG"

    def __init__(self):
        self.q = {e: [] for e in self.ENGS}
        self.cnt = {}

    def reg(self, sem):
        self.cnt[sem] = 0
        return sem

    def emit(self, e, fn, waits=(), inc=None, n=1):
        flat = []
        for w in waits:
            if w is None:
                continue
            if isinstance(w, list):
                flat.extend(x for x in w if x is not None)
            else:
                flat.append(w)
        ws = tuple(flat)
        for (s, v) in ws:
            if v > self.cnt[s]:
                raise RuntimeError(f"wait-on-future: {v} > {self.cnt[s]}")
        tok = None
        if inc is not None:
            self.cnt[inc] += n
            tok = (inc, self.cnt[inc])
        self.q[e].append((fn, ws, inc, n))
        return tok


class LoadRing:
    def __init__(self, p, sems):
        self.p = p
        self.sems = sems
        self.n = len(sems)
        self.k = 0
        self.rel = [None] * self.n
        self.base = {}

    def load(self, eng, fnmk, extra_waits=(), ndma=1):
        sl = self.k % self.n
        w = list(extra_waits)
        if self.rel[sl] is not None:
            w.append(self.rel[sl])
        self.base.setdefault(sl, 0)
        for i in range(ndma):
            fn = fnmk(sl) if ndma == 1 else fnmk(sl, i)
            self.p.emit(eng, fn, waits=(w if i == 0 else ()),
                        inc=self.sems[sl], n=16)
        self.base[sl] += 16 * ndma
        tok = (self.sems[sl], self.base[sl])
        idx = self.k
        self.k += 1
        return sl, idx, tok

    def release(self, idx, tok):
        self.rel[idx % self.n] = tok


class StoreRing:
    def __init__(self, p, sems):
        self.p = p
        self.sems = sems
        self.n = len(sems)
        self.k = 0
        self.base = {}

    def next(self, ndma=1):
        sl = self.k % self.n
        prev = self.base.get(sl, 0)
        stw = (self.sems[sl], prev) if prev > 0 else None
        self.base[sl] = prev + 16 * ndma
        self.k += 1
        return sl, stw, (self.sems[sl], self.base[sl])


class Bank:
    def __init__(self, t):
        self.t = t
        self.rel = None


def build_program():
    nc = bass.Bass("TRN2", target_bir_lowering=False)

    xin = nc.dram_tensor("xin", [DC, 128, T], F32, kind="ExternalInput")
    w1s = nc.dram_tensor("w1s", [OC, 128, D], F32, kind="ExternalInput")
    w3s = nc.dram_tensor("w3s", [OC, 128, D], F32, kind="ExternalInput")
    w2t = nc.dram_tensor("w2t", [8192, D], F32, kind="ExternalInput")
    wqs = nc.dram_tensor("wqs", [48, 128, D], F32, kind="ExternalInput")
    wvs4 = nc.dram_tensor("wvs4", [4, 128, 8192], F32, kind="ExternalInput")
    wos = nc.dram_tensor("wos", [DC, 128, D], F32, kind="ExternalInput")
    bq = nc.dram_tensor("bq", [128, 48], F32, kind="ExternalInput")
    bvr = nc.dram_tensor("bvr", [1, D], F32, kind="ExternalInput")
    bo = nc.dram_tensor("bo", [128, DC], F32, kind="ExternalInput")
    h128 = nc.dram_tensor("h128", [128, 128], BF16, kind="ExternalInput")
    h64 = nc.dram_tensor("h64", [64, 64], BF16, kind="ExternalInput")
    masks = nc.dram_tensor("masks", [2, 128, 128], BF16, kind="ExternalInput")
    ident = nc.dram_tensor("ident", [128, 128], BF16, kind="ExternalInput")

    stream = nc.dram_tensor("stream", [DC, 128, T], F32)
    m1 = nc.dram_tensor("m1", [OC, 128, D], BF16)
    m2b = nc.dram_tensor("m2b", [OC, 128, D], BF16)
    k_own = nc.dram_tensor("k_own", [NH, 128, T], BF16)
    v_own = nc.dram_tensor("v_own", [NH, QT, 128, 128], BF16)
    k_ag = nc.dram_tensor("k_ag", [2 * NH, 128, T], BF16)
    v_ag = nc.dram_tensor("v_ag", [2 * NH, QT, 128, 128], BF16)

    xout = nc.dram_tensor("xout", [DC, 128, T], F32, kind="ExternalOutput")

    with ExitStack() as ctx:
        sbt = lambda name, shape, dt: ctx.enter_context(nc.sbuf_tensor(name, shape, dt))
        pst = lambda name: ctx.enter_context(nc.psum_tensor(name, [128, 512], F32))
        sem = lambda name: ctx.enter_context(nc.semaphore(name))

        A = sbt("A", [128, DC * T], BF16)
        B = sbt("Bb", [128, DC * T], BF16)
        wr = sbt("wr", [128, WRN * 2048], BF16)
        fr = sbt("fr", [128, FRN * 512], F32)
        sgf = sbt("sgf", [128, SGFN * 512], F32)
        sgb = sbt("sgb", [128, SGBN * 512], BF16)
        xb16 = sbt("xb16", [128, XBN * 512], BF16)
        s1 = sbt("s1", [128, S1N * 512], F32)
        lnb = sbt("lnb", [128, 1024], F32)
        rcb = sbt("rcb", [128, RBN * 512], F32)
        pt = sbt("pt", [128, PTN * 512], BF16)
        msk = sbt("msk", [128, 256], BF16)
        id_sb = sbt("id_sb", [128, 128], BF16)
        h128_sb = sbt("h128_sb", [128, 128], BF16)
        h64_sb = sbt("h64_sb", [64, 64], BF16)
        onec = sbt("onec", [128, 1], BF16)
        oner = sbt("oner", [1, 128], BF16)
        onerf = sbt("onerf", [1, 128], F32)
        epsr = sbt("epsr", [1, 1], F32)
        bq_sb = sbt("bq_sb", [128, 48], F32)
        bo_sb = sbt("bo_sb", [128, DC], F32)
        bvr_sb = sbt("bvr_sb", [1, D], BF16)
        rows = sbt("rows", [1, 1536], F32)
        h64d = sbt("h64d", [128, 64], BF16)

        banks = [Bank(pst(f"pb{i}")) for i in range(8)]

        P = Prog()
        s_t = P.reg(sem("s_t"))
        s_v = P.reg(sem("s_v"))
        s_s = P.reg(sem("s_s"))
        s_cck = P.reg(sem("s_cck"))
        s_ccv = P.reg(sem("s_ccv"))
        s_rpk = P.reg(sem("s_rpk"))
        s_cst = P.reg(sem("s_cst"))
        s_cstg = P.reg(sem("s_cstg"))
        wr_ring = LoadRing(P, [P.reg(sem(f"swr{i}")) for i in range(WRN)])
        fr_ring = LoadRing(P, [P.reg(sem(f"sfr{i}")) for i in range(FRN)])
        m1_ring = LoadRing(P, [P.reg(sem(f"sm1{i}")) for i in range(4)])
        sgf_ring = StoreRing(P, [P.reg(sem(f"ssf{i}")) for i in range(SGFN)])
        pst_ring = StoreRing(P, [P.reg(sem(f"sps{i}")) for i in range(4)])
        sgb_ring = StoreRing(P, [P.reg(sem(f"ssb{i}")) for i in range(SGBN)])

        tile_store = {}
        tile_loads = {}

        def t_emit(fn, waits=(), last=False):
            return P.emit('T', fn, waits=waits, inc=(s_t if last else None))

        def barrier():
            tt = (s_t, P.cnt[s_t])
            tv = (s_v, P.cnt[s_v])
            ts = (s_s, P.cnt[s_s])
            P.emit('T', lambda e: e.nop(), waits=[tv, ts])
            P.emit('V', lambda e: e.nop(), waits=[tt, ts])
            P.emit('S', lambda e: e.nop(), waits=[tt, tv])

        def consts():
            P.emit('V', lambda e: e.memset(onec[:, :], 1.0), inc=s_v)
            P.emit('V', lambda e: e.memset(oner[:, :], 1.0), inc=s_v)
            P.emit('V', lambda e: e.memset(onerf[:, :], 1.0), inc=s_v)
            P.emit('V', lambda e: e.memset(epsr[:, :], EPS), inc=s_v)
            P.emit('Y', lambda e: e.dma_start(out=h128_sb[:, :], in_=h128[:, :]), inc=s_cst, n=16)
            P.emit('Y', lambda e: e.dma_start(out=h64_sb[:, :], in_=h64[:, :]), inc=s_cst, n=16)
            P.emit('Y', lambda e: e.dma_start(out=h64d[0:64, :], in_=h64[:, :]), inc=s_cst, n=16)
            P.emit('Y', lambda e: e.dma_start(out=h64d[64:128, :], in_=h64[:, :]), inc=s_cst, n=16)
            P.emit('Y', lambda e: e.dma_start(out=msk[:, 0:128], in_=masks[0, :, :]), inc=s_cst, n=16)
            P.emit('Y', lambda e: e.dma_start(out=id_sb[:, :], in_=ident[:, :]), inc=s_cst, n=16)
            P.emit('Y', lambda e: e.dma_start(out=msk[:, 128:256], in_=masks[1, :, :]), inc=s_cst, n=16)
            P.emit('Y', lambda e: e.dma_start(out=bq_sb[:, :], in_=bq[:, :]), inc=s_cst, n=16)
            P.emit('Y', lambda e: e.dma_start(out=bo_sb[:, :], in_=bo[:, :]), inc=s_cst, n=16)
            P.emit('G', lambda e: e.dma_start(out=bvr_sb[:, :], in_=bvr[:, :]), inc=s_cstg, n=16)
            return (s_cst, 9 * 16), (s_v, P.cnt[s_v])

        cst_tok, ones_tok = consts()

        def prologue_pass1(hcs):
            for hc in hcs:
                sl, idx, wtok = wr_ring.load(
                    'G', lambda s, hc=hc: (lambda e, s=s, hc=hc: e.dma_start(
                        out=wr[:, s * 2048:(s + 1) * 2048],
                        in_=w2t[hc * 128:(hc + 1) * 128, :])))
                ssl, stw, sttok = pst_ring.next()
                etoks = []
                for dq in range(4):
                    bk = banks[dq % 2]
                    tok = t_emit(
                        lambda e, s=sl, dq=dq, bk=bk: e.matmul(
                            bk.t[:, :], h128_sb[:, :],
                            wr[:, s * 2048 + dq * 512: s * 2048 + (dq + 1) * 512],
                            start=True, stop=True),
                        waits=[wtok, cst_tok, bk.rel], last=True)
                    if dq % 2 == 0:
                        etok = P.emit('S', lambda e, ssl=ssl, dq=dq, bk=bk: e.copy(
                            B[:, 24576 + ssl * 2048 + dq * 512: 24576 + ssl * 2048 + (dq + 1) * 512],
                            bk.t[:, :]),
                            waits=[tok, stw], inc=s_s)
                    else:
                        etok = P.emit('V', lambda e, ssl=ssl, dq=dq, bk=bk: e.tensor_copy(
                            B[:, 24576 + ssl * 2048 + dq * 512: 24576 + ssl * 2048 + (dq + 1) * 512],
                            bk.t[:, :]),
                            waits=[tok, stw], inc=s_v)
                    bk.rel = etok
                    etoks.append(etok)
                P.emit('S', lambda e, ssl=ssl, hc=hc: e.dma_start(
                    out=m1[hc, :, :], in_=B[:, 24576 + ssl * 2048: 24576 + (ssl + 1) * 2048]),
                    waits=etoks, inc=pst_ring.sems[ssl], n=16)
                wr_ring.release(idx, (s_t, P.cnt[s_t]))

        def prologue_pass2(pps, m1_done):
            for p in pps:
                sl, idx, wtok = m1_ring.load(
                    'Y', lambda s, half, p=p: (lambda e, s=s, half=half, p=p: e.dma_start(
                        out=B[half * 64:(half + 1) * 64, 16384 + s * D: 16384 + (s + 1) * D],
                        in_=m1[:, p + half, :])),
                    extra_waits=m1_done, ndma=2)
                ssl, stw, sttok = pst_ring.next()
                etoks = []
                for dq in range(4):
                    bk = banks[6 + dq % 2]
                    t_emit(lambda e, s=sl, dq=dq, bk=bk: e.matmul(
                        bk.t[0:64, :], h64d[0:64, :],
                        B[0:64, 16384 + s * D + dq * 512: 16384 + s * D + (dq + 1) * 512],
                        start=True, stop=True),
                        waits=[wtok, cst_tok, bk.rel], last=True)
                    tok = t_emit(lambda e, s=sl, dq=dq, bk=bk: e.matmul(
                        bk.t[64:128, :], h64d[64:128, :],
                        B[64:128, 16384 + s * D + dq * 512: 16384 + s * D + (dq + 1) * 512],
                        start=True, stop=True),
                        waits=[wtok], last=True)
                    if dq % 2 == 0:
                        etok = P.emit('S', lambda e, ssl=ssl, dq=dq, bk=bk: e.activation(
                            B[:, 24576 + ssl * 2048 + dq * 512: 24576 + ssl * 2048 + (dq + 1) * 512],
                            bk.t[:, :], AF.Identity, scale=WHS),
                            waits=[tok, stw], inc=s_s)
                    else:
                        etok = P.emit('V', lambda e, ssl=ssl, dq=dq, bk=bk: e.tensor_scalar_mul(
                            B[:, 24576 + ssl * 2048 + dq * 512: 24576 + ssl * 2048 + (dq + 1) * 512],
                            bk.t[:, :], WHS),
                            waits=[tok, stw], inc=s_v)
                    bk.rel = etok
                    etoks.append(etok)
                P.emit('S', lambda e, ssl=ssl, p=p: e.dma_start(
                    out=m2b[:, p:p + 2, :].rearrange("hc pp d -> pp hc d"),
                    in_=B[:, 24576 + ssl * 2048: 24576 + (ssl + 1) * 2048].rearrange(
                        "q (dq j) -> q dq j", j=512)),
                    waits=etoks, inc=pst_ring.sems[ssl], n=16)
                m1_ring.release(idx, (s_t, P.cnt[s_t]))

        s1_rel = [None] * S1N

        def ln_phase(src, dstbuf, dst_war=None):
            return [ln_tb(src, dstbuf, tb, dst_war) for tb in range(NTQ)]

        def ln_tb(src, dstbuf, tb, dst_war=None):
            if True:
                bsum, bsq = banks[4], banks[5]
                prev = []
                for ic in range(DC):
                    sl, idx, wtok = fr_ring.load(
                        'Y', lambda s, ic=ic, tb=tb, src=src: (
                            lambda e, s=s, ic=ic, tb=tb, src=src: e.dma_start(
                                out=fr[:, s * 512:(s + 1) * 512],
                                in_=src[ic, :, tb * 512:(tb + 1) * 512])),
                        extra_waits=[tile_store.get((src.name, ic, tb))])
                    tile_loads.setdefault((src.name, ic, tb), []).append(wtok)
                    xsl = ic % XBN
                    ctok = P.emit('S', lambda e, sl=sl, xsl=xsl: e.copy(
                        xb16[:, xsl * 512:(xsl + 1) * 512],
                        fr[:, sl * 512:(sl + 1) * 512]),
                        waits=[wtok, prev[ic - XBN][1] if ic >= XBN else None], inc=s_s)
                    t1 = t_emit(lambda e, xsl=xsl, ic=ic, bsum=bsum: e.matmul(
                        bsum.t[0:1, :], onec[:, :],
                        xb16[:, xsl * 512:(xsl + 1) * 512],
                        start=(ic == 0), stop=(ic == DC - 1)),
                        waits=[ctok, ones_tok, bsum.rel if ic == 0 else None],
                        last=True)
                    qtok = P.emit('S', lambda e, sl=sl, xsl=xsl: e.activation(
                        xb16[:, xsl * 512:(xsl + 1) * 512],
                        fr[:, sl * 512:(sl + 1) * 512], AF.Square),
                        waits=[t1], inc=s_s)
                    t2 = t_emit(lambda e, xsl=xsl, ic=ic, bsq=bsq: e.matmul(
                        bsq.t[0:1, :], onec[:, :],
                        xb16[:, xsl * 512:(xsl + 1) * 512],
                        start=(ic == 0), stop=(ic == DC - 1)),
                        waits=[qtok, bsq.rel if ic == 0 else None],
                        last=True)
                    fr_ring.release(idx, qtok)
                    prev.append((t1, t2))
                sum_tok = prev[DC - 1][0]
                sq_tok = prev[DC - 1][1]
                r_mean = rows[0:1, 0:512]
                r_tmp = rows[0:1, 512:1024]
                r_rstd = rows[0:1, 1024:1536]
                r_nmr = rows[0:1, 512:1024]
                tok = P.emit('S', lambda e, bsum=bsum: e.mul(r_mean, bsum.t[0:1, :], 1.0 / D),
                             waits=[sum_tok], inc=s_s)
                bsum.rel = tok
                tok2 = P.emit('S', lambda e: e.activation(r_tmp, r_mean, AF.Square),
                              waits=[tok], inc=s_s)
                tok3 = P.emit('V', lambda e, bsq=bsq: e.scalar_tensor_tensor(
                    r_tmp, bsq.t[0:1, :], 1.0 / D, r_tmp, ALU.mult, ALU.subtract),
                    waits=[tok2, sq_tok], inc=s_v)
                bsq.rel = tok3
                tok4 = P.emit('S', lambda e: e.activation(r_tmp, r_tmp, AF.Ln, bias=epsr[:, :]),
                              waits=[tok3, cst_tok], inc=s_s)
                tok5 = P.emit('S', lambda e: e.activation(r_rstd, r_tmp, AF.Exp, scale=-0.5),
                              waits=[tok4], inc=s_s)
                tok6 = P.emit('V', lambda e: e.tensor_mul(r_nmr, r_mean, r_rstd),
                              waits=[tok5], inc=s_v)
                tok7 = P.emit('S', lambda e: e.mul(r_nmr, r_nmr, -1.0), waits=[tok6], inc=s_s)
                bbc = banks[6]
                t1b = t_emit(lambda e, bbc=bbc: e.matmul(
                    bbc.t[:, :], onerf[:, :], r_rstd, start=True, stop=True),
                    waits=[tok5, ones_tok, bbc.rel], last=True)
                e1 = P.emit('S', lambda e, bbc=bbc: e.copy(lnb[:, 0:512], bbc.t[:, :]),
                            waits=[t1b], inc=s_s)
                bbc.rel = e1
                t2b = t_emit(lambda e, bbc=bbc: e.matmul(
                    bbc.t[:, :], onerf[:, :], r_nmr, start=True, stop=True),
                    waits=[tok7, bbc.rel], last=True)
                e2 = P.emit('S', lambda e, bbc=bbc: e.copy(lnb[:, 512:1024], bbc.t[:, :]),
                            waits=[t2b], inc=s_s)
                bbc.rel = e2
                for ic in range(DC):
                    sl, idx, wtok = fr_ring.load(
                        'Y', lambda s, ic=ic, tb=tb, src=src: (
                            lambda e, s=s, ic=ic, tb=tb, src=src: e.dma_start(
                                out=fr[:, s * 512:(s + 1) * 512],
                                in_=src[ic, :, tb * 512:(tb + 1) * 512])),
                        extra_waits=[tile_store.get((src.name, ic, tb))])
                    tile_loads.setdefault((src.name, ic, tb), []).append(wtok)
                    ssl = sl % S1N
                    mtok = P.emit('V', lambda e, sl=sl, ssl=ssl: e.tensor_mul(
                        s1[:, ssl * 512:(ssl + 1) * 512],
                        fr[:, sl * 512:(sl + 1) * 512], lnb[:, 0:512]),
                        waits=[wtok, e1, s1_rel[ssl]], inc=s_v)
                    atok = P.emit('V', lambda e, sl=sl, ssl=ssl, ic=ic, tb=tb, dstbuf=dstbuf: e.tensor_add(
                        dstbuf[:, ic * T + tb * 512: ic * T + (tb + 1) * 512],
                        s1[:, ssl * 512:(ssl + 1) * 512], lnb[:, 512:1024]),
                        waits=[mtok, e2, dst_war], inc=s_v)
                    s1_rel[ssl] = atok
                    fr_ring.release(idx, atok)
            return atok

        def ffn_run(lnbuf, gbuf, src0, dst, w2ht_ready_fn, tb_toks=None, gwar0=None, aux=None):
            state = {}
            state[('gwar', 0)] = gwar0
            hu = [0]
            yu = [0]
            for og in range(NOG):
                g_tok_last = None
                for oc16 in range(16):
                    oc = og * 16 + oc16
                    sl1, idx1, wtok1 = wr_ring.load(
                        'G', lambda s, oc=oc: (lambda e, s=s, oc=oc: e.dma_start(
                            out=wr[:, s * 2048:(s + 1) * 2048], in_=w1s[oc, :, :])))
                    sl3, idx3, wtok3 = wr_ring.load(
                        'G', lambda s, oc=oc: (lambda e, s=s, oc=oc: e.dma_start(
                            out=wr[:, s * 2048:(s + 1) * 2048], in_=w3s[oc, :, :])))
                    for tq in range(NTQ):
                        u = hu[0]
                        bA = banks[(u % 3) * 2]
                        bH = banks[(u % 3) * 2 + 1]
                        tokh1 = None
                        for ic in range(ICN):
                            tokh1 = t_emit(
                                lambda e, sl=sl1, ic=ic, tq=tq, bk=bA, lnbuf=lnbuf,
                                st=(ic == 0), sp=(ic == ICN - 1): e.matmul(
                                    bk.t[:, :],
                                    wr[:, sl * 2048 + ic * 128: sl * 2048 + (ic + 1) * 128],
                                    lnbuf[:, ic * T + tq * 512: ic * T + (tq + 1) * 512],
                                    start=st, stop=sp),
                                waits=[wtok1, bA.rel if ic == 0 else None,
                                       tb_toks[tq] if (ic == 0 and tb_toks) else None],
                                last=(ic == ICN - 1))
                        tokh3 = None
                        for ic in range(ICN):
                            tokh3 = t_emit(
                                lambda e, sl=sl3, ic=ic, tq=tq, bk=bH, lnbuf=lnbuf,
                                st=(ic == 0), sp=(ic == ICN - 1): e.matmul(
                                    bk.t[:, :],
                                    wr[:, sl * 2048 + ic * 128: sl * 2048 + (ic + 1) * 128],
                                    lnbuf[:, ic * T + tq * 512: ic * T + (tq + 1) * 512],
                                    start=st, stop=sp),
                                waits=[wtok3, bH.rel if ic == 0 else None],
                                last=(ic == ICN - 1))
                        stok = P.emit('S', lambda e, u=u, bk=bA: e.activation(
                            s1[:, (u % S1N) * 512:(u % S1N + 1) * 512], bk.t[:, :],
                            AF.Silu),
                            waits=[tokh1, s1_rel[u % S1N]], inc=s_s)
                        bA.rel = stok
                        gtok = P.emit('V', lambda e, u=u, oc16=oc16, tq=tq, bk=bH, gbuf=gbuf: e.tensor_mul(
                            gbuf[:, oc16 * T + tq * 512: oc16 * T + (tq + 1) * 512],
                            s1[:, (u % S1N) * 512:(u % S1N + 1) * 512], bk.t[:, :]),
                            waits=[stok, tokh3, state.get(('gwar', og)),
                                   state.get('prol_final') if (og == 0 and oc16 >= 8) else None],
                            inc=s_v)
                        bH.rel = gtok
                        s1_rel[u % S1N] = gtok
                        g_tok_last = gtok
                        hu[0] += 1
                    wr_ring.release(idx1, (s_t, P.cnt[s_t]))
                    wr_ring.release(idx3, (s_t, P.cnt[s_t]))
                    if aux is not None and og == 0 and oc16 < 8:
                        aux(oc16)
                        if oc16 == 7:
                            state['prol_final'] = (
                                [(sm, P.cnt[sm]) for sm in pst_ring.sems]
                                + [(sm, P.cnt[sm]) for sm in m1_ring.sems])
                if og == NOG - 1:
                    state['last_h'] = (s_t, P.cnt[s_t])
                for dc in range(DC):
                    def _w2load(s, q4, og=og, dc=dc):
                        def go(e, s=s, q4=q4, og=og, dc=dc):
                            return e.dma_start(
                                out=wr[:, s * 2048 + q4 * 512:
                                       s * 2048 + (q4 + 1) * 512].rearrange(
                                    "p (oc j) -> p oc j", j=128),
                                in_=m2b[og * 16 + q4 * 4:og * 16 + (q4 + 1) * 4,
                                        :, dc * 128:(dc + 1) * 128].rearrange(
                                    "oc p j -> p oc j"))
                        return go
                    slw, idxw, wtokw = wr_ring.load('G', _w2load,
                                                    extra_waits=w2ht_ready_fn(),
                                                    ndma=4)
                    for tq in range(NTQ):
                        bk = banks[6 + yu[0] % 2]
                        tok = None
                        for k in range(16):
                            tok = t_emit(
                                lambda e, sl=slw, k=k, tq=tq, bk=bk, gbuf=gbuf,
                                st=(k == 0), sp=(k == 15): e.matmul(
                                    bk.t[:, :],
                                    wr[:, sl * 2048 + k * 128: sl * 2048 + (k + 1) * 128],
                                    gbuf[:, k * T + tq * 512: k * T + (tq + 1) * 512],
                                    start=st, stop=sp),
                                waits=[wtokw, g_tok_last if k == 0 else None,
                                       bk.rel if k == 0 else None],
                                last=(k == 15))
                        rsrc = src0 if og == 0 else stream
                        rdst = dst if og == NOG - 1 else stream
                        slr, idxr, rtok = fr_ring.load(
                            'Y', lambda s, rsrc=rsrc, dc=dc, tq=tq: (
                                lambda e, s=s, rsrc=rsrc, dc=dc, tq=tq: e.dma_start(
                                    out=fr[:, s * 512:(s + 1) * 512],
                                    in_=rsrc[dc, :, tq * 512:(tq + 1) * 512])),
                            extra_waits=[tile_store.get((rsrc.name, dc, tq))])
                        tile_loads.setdefault((rsrc.name, dc, tq), []).append(rtok)
                        ssl, stw, sttok = sgf_ring.next()
                        etok = P.emit('V', lambda e, ssl=ssl, slr=slr, bk=bk: e.scalar_tensor_tensor(
                            sgf[:, ssl * 512:(ssl + 1) * 512], bk.t[:, :], 0.5,
                            fr[:, slr * 512:(slr + 1) * 512], ALU.mult, ALU.add),
                            waits=[tok, rtok, stw], inc=s_v)
                        bk.rel = etok
                        fr_ring.release(idxr, etok)
                        ldtoks = tile_loads.pop((rdst.name, dc, tq), [])
                        P.emit('S', lambda e, ssl=ssl, rdst=rdst, dc=dc, tq=tq: e.dma_start(
                            out=rdst[dc, :, tq * 512:(tq + 1) * 512],
                            in_=sgf[:, ssl * 512:(ssl + 1) * 512]),
                            waits=[etok] + ldtoks, inc=sgf_ring.sems[ssl], n=16)
                        tile_store[(rdst.name, dc, tq)] = sttok
                        yu[0] += 1
                    wr_ring.release(idxw, (s_t, P.cnt[s_t]))
                state[('gwar', og + 1)] = (s_t, P.cnt[s_t])
                if og == NOG - 1:
                    pass
                if og == NOG - 2:
                    pass
            state['last_y'] = state[('gwar', NOG)]
            return state

        def qkv(lnbuf, qbuf, q_war, tb_toks=None):
            for h in range(NH):
                oc = 16 + h
                slw, idxw, wtokw = wr_ring.load(
                    'G', lambda s, oc=oc: (lambda e, s=s, oc=oc: e.dma_start(
                        out=wr[:, s * 2048:(s + 1) * 2048], in_=wqs[oc, :, :])))
                for tq in range(NTQ):
                    bk = banks[(h * NTQ + tq) % 6]
                    tok = None
                    for ic in range(ICN):
                        tok = t_emit(
                            lambda e, sl=slw, ic=ic, tq=tq, bk=bk, lnbuf=lnbuf,
                            st=(ic == 0), sp=(ic == ICN - 1): e.matmul(
                                bk.t[:, :],
                                wr[:, sl * 2048 + ic * 128: sl * 2048 + (ic + 1) * 128],
                                lnbuf[:, ic * T + tq * 512: ic * T + (tq + 1) * 512],
                                start=st, stop=sp),
                            waits=[wtokw, bk.rel if ic == 0 else None,
                                   tb_toks[tq] if (ic == 0 and tb_toks) else None],
                            last=(ic == ICN - 1))
                    ssl, stw, sttok = sgb_ring.next()
                    etok = P.emit('S', lambda e, ssl=ssl, bk=bk, oc=oc: e.activation(
                        sgb[:, ssl * 512:(ssl + 1) * 512], bk.t[:, :], AF.Identity,
                        bias=bq_sb[:, oc:oc + 1]),
                        waits=[tok, stw, cst_tok], inc=s_s)
                    bk.rel = etok
                    P.emit('S', lambda e, ssl=ssl, h=h, tq=tq: e.dma_start(
                        out=k_own[h, :, tq * 512:(tq + 1) * 512],
                        in_=sgb[:, ssl * 512:(ssl + 1) * 512]),
                        waits=[etok], inc=sgb_ring.sems[ssl], n=16)
                wr_ring.release(idxw, (s_t, P.cnt[s_t]))
            cck = P.emit('G', lambda e: e.collective_compute(
                "AllGather", ALU.bypass,
                replica_groups=[[0, 1], [2, 3], [4, 5], [6, 7]],
                ins=[k_own.ap().opt()], outs=[k_ag.ap().opt()]),
                waits=[(sm, P.cnt[sm]) for sm in sgb_ring.sems], inc=s_cck)
            for g4 in range(4):
                slots4 = []
                for q in range(4):
                    slw, idxw, wtokw = wr_ring.load(
                        'G', lambda s, g4=g4, q=q: (lambda e, s=s, g4=g4, q=q: e.dma_start(
                            out=wr[:, s * 2048:(s + 1) * 2048],
                            in_=wvs4[g4, :, q * 2048:(q + 1) * 2048])))
                    slots4.append((slw, idxw, wtokw))
                vwt = [sl[2] for sl in slots4]
                for tt in range(QT):
                    bk = banks[6 + tt % 2]
                    tok = None
                    for ic in range(ICN):
                        vsl = slots4[ic // 4][0]
                        tok = t_emit(
                            lambda e, sl=vsl, ic=ic, tt=tt, bk=bk, lnbuf=lnbuf,
                            st=(ic == 0): e.matmul(
                                bk.t[:, :],
                                lnbuf[:, ic * T + tt * 128: ic * T + (tt + 1) * 128],
                                wr[:, sl * 2048 + (ic % 4) * 512: sl * 2048 + (ic % 4 + 1) * 512],
                                start=st, stop=False),
                            waits=[vwt[ic // 4], bk.rel if ic == 0 else None,
                                   tb_toks[tt // 4] if (ic == 0 and tb_toks) else None])
                    tok = t_emit(lambda e, bk=bk, g4=g4: e.matmul(
                        bk.t[:, :], oner[:, :],
                        bvr_sb[:, g4 * 512:(g4 + 1) * 512],
                        start=False, stop=True),
                        waits=[ones_tok, (s_cstg, 16)], last=True)
                    ssl, stw, sttok = sgb_ring.next()
                    etok = P.emit('S', lambda e, ssl=ssl, bk=bk: e.copy(
                        sgb[:, ssl * 512:(ssl + 1) * 512], bk.t[:, :]),
                        waits=[tok, stw], inc=s_s)
                    bk.rel = etok
                    P.emit('S', lambda e, ssl=ssl, g4=g4, tt=tt: e.dma_start(
                        out=v_own[g4 * 4:(g4 + 1) * 4, tt, :, :].rearrange(
                            "he p ee -> p he ee"),
                        in_=sgb[:, ssl * 512:(ssl + 1) * 512].rearrange(
                            "p (he ee) -> p he ee", ee=128)),
                        waits=[etok], inc=sgb_ring.sems[ssl], n=16)
                for (slw, idxw, _) in slots4:
                    wr_ring.release(idxw, (s_t, P.cnt[s_t]))
            ccv = P.emit('G', lambda e: e.collective_compute(
                "AllGather", ALU.bypass,
                replica_groups=[[0, 1], [2, 3], [4, 5], [6, 7]],
                ins=[v_own.ap().opt()], outs=[v_ag.ap().opt()]),
                waits=[(sm, P.cnt[sm]) for sm in sgb_ring.sems], inc=s_ccv)
            for h in range(NH):
                oc = h
                slw, idxw, wtokw = wr_ring.load(
                    'G', lambda s, oc=oc: (lambda e, s=s, oc=oc: e.dma_start(
                        out=wr[:, s * 2048:(s + 1) * 2048], in_=wqs[oc, :, :])))
                for tq in range(NTQ):
                    bk = banks[(h * NTQ + tq) % 6]
                    tok = None
                    for ic in range(ICN):
                        tok = t_emit(
                            lambda e, sl=slw, ic=ic, tq=tq, bk=bk, lnbuf=lnbuf,
                            st=(ic == 0), sp=(ic == ICN - 1): e.matmul(
                                bk.t[:, :],
                                wr[:, sl * 2048 + ic * 128: sl * 2048 + (ic + 1) * 128],
                                lnbuf[:, ic * T + tq * 512: ic * T + (tq + 1) * 512],
                                start=st, stop=sp),
                            waits=[wtokw, bk.rel if ic == 0 else None,
                                   tb_toks[tq] if (ic == 0 and tb_toks) else None],
                            last=(ic == ICN - 1))
                    etok = P.emit('S', lambda e, bk=bk, h=h, tq=tq, oc=oc, qbuf=qbuf: e.activation(
                        qbuf[:, h * T + tq * 512: h * T + (tq + 1) * 512],
                        bk.t[:, :], AF.Identity, bias=bq_sb[:, oc:oc + 1]),
                        waits=[tok, cst_tok, q_war], inc=s_s)
                    bk.rel = etok
                wr_ring.release(idxw, (s_t, P.cnt[s_t]))
            return (s_cck, P.cnt[s_cck]), (s_ccv, P.cnt[s_ccv]), (s_t, P.cnt[s_t])

        def attention(qbuf, obuf, cck, ccv, o_war):
            q_ready = (s_s, P.cnt[s_s])
            pt_rel = [None] * PTN
            rcb_rel = [None] * RBN
            uu = [0]
            o_last = None
            div_pending = []

            def div_flush():
                bl, bO, ltok, ottok, h, g = div_pending.pop(0)
                r_l = rows[0:1, 0:512]
                r_rc = rows[0:1, 512:1024]
                t1 = P.emit('S', lambda e, bl=bl: e.activation(
                    r_l, bl.t[0:1, :], AF.Ln), waits=[ltok], inc=s_s)
                bl.rel = t1
                t2 = P.emit('S', lambda e: e.activation(r_rc, r_l, AF.Exp, scale=-1.0),
                            waits=[t1], inc=s_s)
                bbc = banks[6]
                t3 = t_emit(lambda e, bbc=bbc: e.matmul(
                    bbc.t[:, :], onerf[:, :], r_rc, start=True, stop=True),
                    waits=[t2, ones_tok, bbc.rel], last=True)
                gg = (h * 4 + g) % RBN
                t4 = P.emit('S', lambda e, bbc=bbc, gg=gg: e.copy(
                    rcb[:, gg * 512:(gg + 1) * 512], bbc.t[:, :]),
                    waits=[t3, rcb_rel[gg]], inc=s_s)
                bbc.rel = t4
                t5 = P.emit('V', lambda e, bk=bO, gg=gg, h=h, g=g, obuf=obuf: e.tensor_mul(
                    obuf[:, h * T + g * 512: h * T + (g + 1) * 512],
                    bk.t[:, :], rcb[:, gg * 512:(gg + 1) * 512]),
                    waits=[t4, ottok, o_war], inc=s_v)
                bO.rel = t5
                rcb_rel[gg] = t5
                return t5
            for h in range(NH):
                slots = []
                for r in range(2):
                    slk, idxk, wtk = wr_ring.load(
                        'G', lambda s, r=r, h=h: (lambda e, s=s, r=r, h=h: e.dma_start(
                            out=wr[:, s * 2048:(s + 1) * 2048],
                            in_=k_ag[r * NH + h, :, :])),
                        extra_waits=[cck])
                    slots.append((slk, idxk, wtk))
                for r in range(2):
                    def _vload(s, q4, r=r, h=h):
                        def go(e, s=s, q4=q4, r=r, h=h):
                            return e.dma_start(
                                out=wr[:, s * 2048 + q4 * 512:
                                       s * 2048 + (q4 + 1) * 512].rearrange(
                                    "p (tt ee) -> p tt ee", ee=128),
                                in_=v_ag[r * NH + h, q4 * 4:(q4 + 1) * 4,
                                         :, :].rearrange("tt p ee -> p tt ee"))
                        return go
                    slv, idxv, wtv = wr_ring.load('G', _vload,
                                                  extra_waits=[ccv], ndma=4)
                    slots.append((slv, idxv, wtv))
                ktoks = [slots[0][2], slots[1][2]]
                vtoks = [slots[2][2], slots[3][2]]
                for g in range(4):
                    if div_pending:
                        o_last = div_flush()
                    bl = banks[4 + g % 2]
                    bO = banks[2 + g % 2]
                    jn = 8 * g + 8
                    ltok = None
                    ottok = None
                    pending = []

                    def flush_one(g=g, jn=jn, bl=bl, bO=bO):
                        nonlocal ltok, ottok
                        j, u, off, ptok = pending.pop(0)
                        ltok = t_emit(lambda e, off=off, u=u, bl=bl,
                                      st=(j == 0), sp=(j == jn - 1): e.matmul(
                            bl.t[0:1, off:512], onec[:, :],
                            pt[:, (u % PTN) * 512 + off:(u % PTN) * 512 + 512],
                            start=st, stop=sp),
                            waits=[ptok, ones_tok, bl.rel if j == 0 else None],
                            last=(j == jn - 1))
                        vap = wr[:, slots[2 + j % 2][0] * 2048 + (j // 2) * 128:
                                 slots[2 + j % 2][0] * 2048 + (j // 2 + 1) * 128]
                        ottok = t_emit(lambda e, vap=vap, off=off, u=u, bk=bO,
                                       st=(j == 0), sp=(j == jn - 1): e.matmul(
                            bk.t[:, off:512], vap,
                            pt[:, (u % PTN) * 512 + off:(u % PTN) * 512 + 512],
                            start=st, stop=sp),
                            waits=[vtoks[j % 2], bO.rel if j == 0 else None],
                            last=(j == jn - 1))
                        pt_rel[u % PTN] = ottok

                    for j in range(jn):
                        off = max(0, j // 2 - 4 * g) * 128
                        u = uu[0]
                        bs = banks[u % 2]
                        kap = wr[:, slots[j % 2][0] * 2048 + (j // 2) * 128:
                                 slots[j % 2][0] * 2048 + (j // 2 + 1) * 128]
                        has_mask = (j >= 8 * g)
                        stok = t_emit(lambda e, kap=kap, g=g, off=off, bs=bs, h=h, qbuf=qbuf,
                                      sp=(not has_mask): e.matmul(
                            bs.t[:, off:512], kap,
                            qbuf[:, h * T + g * 512 + off: h * T + (g + 1) * 512],
                            start=True, stop=sp),
                            waits=[ktoks[j % 2], q_ready, bs.rel], last=(not has_mask))
                        mwait = stok
                        if has_mask:
                            mc = 0 if j % 2 == 0 else 128
                            mwait = t_emit(lambda e, bs=bs, off=off, mc=mc: e.matmul(
                                bs.t[:, off:off + 128], id_sb[:, :],
                                msk[:, mc:mc + 128], start=False, stop=True),
                                waits=[cst_tok], last=True)
                        ptok = P.emit('S', lambda e, bs=bs, off=off, u=u: e.activation(
                            pt[:, (u % PTN) * 512 + off:(u % PTN) * 512 + 512],
                            bs.t[:, off:512], AF.Exp, scale=SQS),
                            waits=[mwait, pt_rel[u % PTN]], inc=s_s)
                        bs.rel = ptok
                        pending.append((j, u, off, ptok))
                        if len(pending) > 2:
                            flush_one()
                        uu[0] += 1
                    while pending:
                        flush_one()
                    div_pending.append((bl, bO, ltok, ottok, h, g))
                for (sl, idx, _) in slots:
                    wr_ring.release(idx, (s_t, P.cnt[s_t]))
            while div_pending:
                o_last = div_flush()
            return o_last, (s_t, P.cnt[s_t])

        def wout(obuf, o_ready):
            for dc in range(DC):
                slw, idxw, wtokw = wr_ring.load(
                    'G', lambda s, dc=dc: (lambda e, s=s, dc=dc: e.dma_start(
                        out=wr[:, s * 2048:(s + 1) * 2048], in_=wos[dc, :, :])))
                for tq in range(NTQ):
                    bk = banks[(dc * NTQ + tq) % 6]
                    tok = None
                    for hc in range(NH):
                        tok = t_emit(
                            lambda e, sl=slw, hc=hc, tq=tq, bk=bk, obuf=obuf,
                            st=(hc == 0), sp=(hc == NH - 1): e.matmul(
                                bk.t[:, :],
                                wr[:, sl * 2048 + hc * 128: sl * 2048 + (hc + 1) * 128],
                                obuf[:, hc * T + tq * 512: hc * T + (tq + 1) * 512],
                                start=st, stop=sp),
                            waits=[wtokw, o_ready if hc == 0 else None,
                                   bk.rel if hc == 0 else None],
                            last=(hc == NH - 1))
                    slr, idxr, rtok = fr_ring.load(
                        'Y', lambda s, dc=dc, tq=tq: (
                            lambda e, s=s, dc=dc, tq=tq: e.dma_start(
                                out=fr[:, s * 512:(s + 1) * 512],
                                in_=stream[dc, :, tq * 512:(tq + 1) * 512])),
                        extra_waits=[tile_store.get((stream.name, dc, tq))])
                    tile_loads.setdefault((stream.name, dc, tq), []).append(rtok)
                    ssl, stw, sttok = sgf_ring.next()
                    etok = P.emit('V', lambda e, ssl=ssl, slr=slr, bk=bk, dc=dc: e.scalar_tensor_tensor(
                        sgf[:, ssl * 512:(ssl + 1) * 512], bk.t[:, :],
                        bo_sb[:, dc:dc + 1],
                        fr[:, slr * 512:(slr + 1) * 512], ALU.add, ALU.add),
                        waits=[tok, rtok, stw, cst_tok], inc=s_v)
                    bk.rel = etok
                    fr_ring.release(idxr, etok)
                    ldtoks = tile_loads.pop((stream.name, dc, tq), [])
                    P.emit('S', lambda e, ssl=ssl, dc=dc, tq=tq: e.dma_start(
                        out=stream[dc, :, tq * 512:(tq + 1) * 512],
                        in_=sgf[:, ssl * 512:(ssl + 1) * 512]),
                        waits=[etok] + ldtoks, inc=sgf_ring.sems[ssl], n=16)
                    tile_store[(stream.name, dc, tq)] = sttok
                wr_ring.release(idxw, (s_t, P.cnt[s_t]))
            return (s_t, P.cnt[s_t])

        tb1 = []
        for i in range(NTQ):
            prologue_pass1(range(i * 16, (i + 1) * 16))
            tb1.append(ln_tb(xin, A, i))
        m1_done = [(sm, P.cnt[sm]) for sm in pst_ring.sems]

        def _pass2_aux(oc16, m1_done=m1_done):
            prologue_pass2(range(oc16 * 16, (oc16 + 1) * 16, 2), m1_done)

        def _w2rdy():
            return [(sm, P.cnt[sm]) for sm in pst_ring.sems]

        st1 = ffn_run(A, B, xin, stream, _w2rdy, tb_toks=tb1, aux=_pass2_aux)
        tb2 = ln_phase(stream, A, dst_war=st1['last_h'])
        cck, ccv, qkv_last = qkv(A, B, st1['last_y'], tb_toks=tb2)
        o_ready, attn_last = attention(B, A, cck, ccv, o_war=qkv_last)
        wout_last = wout(A, o_ready)
        tb3 = ln_phase(stream, B, dst_war=attn_last)
        ffn_run(B, A, stream, xout, lambda: [], tb_toks=tb3, gwar0=wout_last)

        fin = [(sm, P.cnt[sm]) for sm in sgf_ring.sems]
        P.emit('Y', lambda e: e.nop(), waits=fin)

        with nc.Block() as block:
            def mk(key):
                def body(eng):
                    seen = {}
                    for (fn, ws, inc, n) in P.q[key]:
                        for (s, v) in ws:
                            if seen.get(s, 0) < v:
                                eng.wait_ge(s, v)
                                seen[s] = v
                        inst = fn(eng)
                        if inc is not None:
                            inst.then_inc(inc, n)
                return body
            block.tensor(mk('T'))
            block.vector(mk('V'))
            block.scalar(mk('S'))
            block.sync(mk('Y'))
            block.gpsimd(mk('G'))

    return nc


_CACHED = {}


def _host_prep(L0_attn_x, L0_attn_W_qkv, L0_attn_b_qkv, L0_attn_W_out,
               L0_attn_b_out, L0_ffn_W1, L0_ffn_W2, L0_ffn_W3):
    x = np.asarray(L0_attn_x, dtype=np.float32)
    W1 = np.asarray(L0_ffn_W1, dtype=np.float32)
    W2 = np.asarray(L0_ffn_W2, dtype=np.float32)
    W3 = np.asarray(L0_ffn_W3, dtype=np.float32)
    Wqkv = np.asarray(L0_attn_W_qkv, dtype=np.float32)
    bqkv = np.asarray(L0_attn_b_qkv, dtype=np.float32)
    Wout = np.asarray(L0_attn_W_out, dtype=np.float32)
    bout = np.asarray(L0_attn_b_out, dtype=np.float32)

    def tile4(w, noc):
        return np.ascontiguousarray(
            w.reshape(noc, 128, 16, 128).transpose(0, 3, 2, 1).reshape(noc, 128, 2048))

    w1s = tile4(W1, 64)
    w3s = tile4(W3, 64)
    wqs = tile4(Wqkv, 48)
    wv = Wqkv[4096:6144]
    wvs4 = np.ascontiguousarray(
        wv.reshape(4, 4, 128, 16, 128).transpose(0, 4, 3, 1, 2).reshape(4, 128, 8192))
    wos = tile4(Wout, 16)
    w2t = np.ascontiguousarray(W2.T)
    bqh = np.ascontiguousarray(bqkv.reshape(48, 128).T)
    bvr = np.ascontiguousarray(bqkv[4096:6144].reshape(1, 2048))
    boh = np.ascontiguousarray(bout.reshape(16, 128).T)
    h128a = np.ascontiguousarray(_hadamard(128)).astype(ml_dtypes.bfloat16)
    h64a = np.ascontiguousarray(_hadamard(64)).astype(ml_dtypes.bfloat16)
    identa = np.eye(128, dtype=np.float32).astype(ml_dtypes.bfloat16)

    diag_mask = np.where(np.arange(128)[:, None] <= np.arange(128)[None, :],
                         0.0, NEGBIG).astype(np.float32)
    full_mask = np.full((128, 128), NEGBIG, dtype=np.float32)
    zero_mask = np.zeros((128, 128), dtype=np.float32)
    masks_by_parity = [
        np.stack([diag_mask, full_mask]),
        np.stack([zero_mask, diag_mask]),
    ]

    in_maps = []
    rows_by_core = []
    for c in range(8):
        b, par = divmod(c, 2)
        tiles = [2 * i + par for i in range(16)]
        rws = np.concatenate([np.arange(g * 128, (g + 1) * 128) for g in tiles])
        rows_by_core.append((b, rws))
        xc = np.ascontiguousarray(x[b].reshape(4096, 2048)[rws].T).reshape(16, 128, 2048)
        in_maps.append({
            "xin": xc, "w1s": w1s, "w3s": w3s, "w2t": w2t, "wqs": wqs,
            "wos": wos, "wvs4": wvs4, "bq": bqh, "bvr": bvr, "bo": boh,
            "h128": h128a, "h64": h64a, "ident": identa,
            "masks": np.ascontiguousarray(masks_by_parity[par]).astype(ml_dtypes.bfloat16),
        })
    return in_maps, rows_by_core


def kernel(L0_attn_x, L0_attn_W_qkv, L0_attn_b_qkv, L0_attn_W_out,
           L0_attn_b_out, L0_ffn_W1, L0_ffn_W2, L0_ffn_W3):
    in_maps, rows_by_core = _host_prep(
        L0_attn_x, L0_attn_W_qkv, L0_attn_b_qkv, L0_attn_W_out,
        L0_attn_b_out, L0_ffn_W1, L0_ffn_W2, L0_ffn_W3)
    if 'nc' not in _CACHED:
        _CACHED['nc'] = build_program()
    nc = _CACHED['nc']
    res = run_bass_kernel_spmd(nc, in_maps, core_ids=list(range(8)))
    out = np.empty((4, 4096, 2048), dtype=np.float32)
    for c in range(8):
        b, rws = rows_by_core[c]
        xo = np.asarray(res.results[c]["xout"]).reshape(2048, 2048)
        out[b].reshape(4096, 2048)[rws] = np.ascontiguousarray(xo.T)
    return out


# version 14 (final)
# speedup vs baseline: 1.1363x; 1.1363x over previous
import numpy as np
import ml_dtypes
from contextlib import ExitStack

import concourse.bass as bass
import concourse.mybir as mybir
from concourse.bass_utils import run_bass_kernel_spmd

F32 = mybir.dt.float32
BF16 = mybir.dt.bfloat16
AF = mybir.ActivationFunctionType
ALU = mybir.AluOpType

D = 2048
T = 2048
NH = 16
OC = 64
DC = 16
ICN = 16
NTQ = 4
QT = 16
NOG = 4
SQS = 0.08838834764831845
WHS = 0.011048543456039806
NEGBIG = -1.0e30
EPS = 1.0e-5

WRN = 7
FRN = 3
SGFN = 4
SGBN = 7
XBN = 3
S1N = 2
PTN = 3
RBN = 1


def _hadamard(n):
    m = np.array([[1.0]], dtype=np.float32)
    while m.shape[0] < n:
        m = np.block([[m, m], [m, -m]])
    return m


class Prog:
    ENGS = "TVSYG"

    def __init__(self):
        self.q = {e: [] for e in self.ENGS}
        self.cnt = {}

    def reg(self, sem):
        self.cnt[sem] = 0
        return sem

    def emit(self, e, fn, waits=(), inc=None, n=1):
        flat = []
        for w in waits:
            if w is None:
                continue
            if isinstance(w, list):
                flat.extend(x for x in w if x is not None)
            else:
                flat.append(w)
        ws = tuple(flat)
        for (s, v) in ws:
            if v > self.cnt[s]:
                raise RuntimeError(f"wait-on-future: {v} > {self.cnt[s]}")
        tok = None
        if inc is not None:
            self.cnt[inc] += n
            tok = (inc, self.cnt[inc])
        self.q[e].append((fn, ws, inc, n))
        return tok


class LoadRing:
    def __init__(self, p, sems):
        self.p = p
        self.sems = sems
        self.n = len(sems)
        self.k = 0
        self.rel = [None] * self.n
        self.base = {}

    def load(self, eng, fnmk, extra_waits=(), ndma=1):
        sl = self.k % self.n
        w = list(extra_waits)
        if self.rel[sl] is not None:
            w.append(self.rel[sl])
        self.base.setdefault(sl, 0)
        for i in range(ndma):
            fn = fnmk(sl) if ndma == 1 else fnmk(sl, i)
            self.p.emit(eng, fn, waits=(w if i == 0 else ()),
                        inc=self.sems[sl], n=16)
        self.base[sl] += 16 * ndma
        tok = (self.sems[sl], self.base[sl])
        idx = self.k
        self.k += 1
        return sl, idx, tok

    def release(self, idx, tok):
        self.rel[idx % self.n] = tok


class StoreRing:
    def __init__(self, p, sems):
        self.p = p
        self.sems = sems
        self.n = len(sems)
        self.k = 0
        self.base = {}

    def next(self, ndma=1):
        sl = self.k % self.n
        prev = self.base.get(sl, 0)
        stw = (self.sems[sl], prev) if prev > 0 else None
        self.base[sl] = prev + 16 * ndma
        self.k += 1
        return sl, stw, (self.sems[sl], self.base[sl])


class Bank:
    def __init__(self, t):
        self.t = t
        self.rel = None


def build_program():
    nc = bass.Bass("TRN2", target_bir_lowering=False)

    xin = nc.dram_tensor("xin", [DC, 128, T], F32, kind="ExternalInput")
    w1s = nc.dram_tensor("w1s", [OC, 128, D], F32, kind="ExternalInput")
    w3s = nc.dram_tensor("w3s", [OC, 128, D], F32, kind="ExternalInput")
    w2t = nc.dram_tensor("w2t", [8192, D], F32, kind="ExternalInput")
    wqs = nc.dram_tensor("wqs", [48, 128, D], F32, kind="ExternalInput")
    wvs4 = nc.dram_tensor("wvs4", [4, 128, 8192], F32, kind="ExternalInput")
    wos = nc.dram_tensor("wos", [DC, 128, D], F32, kind="ExternalInput")
    bq = nc.dram_tensor("bq", [128, 48], F32, kind="ExternalInput")
    bvr = nc.dram_tensor("bvr", [1, D], F32, kind="ExternalInput")
    bo = nc.dram_tensor("bo", [128, DC], F32, kind="ExternalInput")
    h128 = nc.dram_tensor("h128", [128, 128], BF16, kind="ExternalInput")
    h64 = nc.dram_tensor("h64", [64, 64], BF16, kind="ExternalInput")
    masks = nc.dram_tensor("masks", [2, 128, 128], BF16, kind="ExternalInput")
    ident = nc.dram_tensor("ident", [128, 128], BF16, kind="ExternalInput")

    stream = nc.dram_tensor("stream", [DC, 128, T], F32)
    m1 = nc.dram_tensor("m1", [OC, 128, D], BF16)
    m2b = nc.dram_tensor("m2b", [OC, 128, D], BF16)
    k_own = nc.dram_tensor("k_own", [NH, 128, T], BF16)
    v_own = nc.dram_tensor("v_own", [NH, QT, 128, 128], BF16)
    k_ag = nc.dram_tensor("k_ag", [2 * NH, 128, T], BF16)
    v_ag = nc.dram_tensor("v_ag", [2 * NH, QT, 128, 128], BF16)

    xout = nc.dram_tensor("xout", [DC, 128, T], F32, kind="ExternalOutput")

    with ExitStack() as ctx:
        sbt = lambda name, shape, dt: ctx.enter_context(nc.sbuf_tensor(name, shape, dt))
        pst = lambda name: ctx.enter_context(nc.psum_tensor(name, [128, 512], F32))
        sem = lambda name: ctx.enter_context(nc.semaphore(name))

        A = sbt("A", [128, DC * T], BF16)
        B = sbt("Bb", [128, DC * T], BF16)
        wr = sbt("wr", [128, WRN * 2048], BF16)
        fr = sbt("fr", [128, FRN * 512], F32)
        sgf = sbt("sgf", [128, SGFN * 512], F32)
        sgb = sbt("sgb", [128, SGBN * 512], BF16)
        xb16 = sbt("xb16", [128, XBN * 512], BF16)
        s1 = sbt("s1", [128, S1N * 512], F32)
        lnb = sbt("lnb", [128, 1024], F32)
        rcb = sbt("rcb", [128, RBN * 512], F32)
        pt = sbt("pt", [128, PTN * 512], BF16)
        msk = sbt("msk", [128, 256], BF16)
        id_sb = sbt("id_sb", [128, 128], BF16)
        h128_sb = sbt("h128_sb", [128, 128], BF16)
        h64_sb = sbt("h64_sb", [64, 64], BF16)
        onec = sbt("onec", [128, 1], BF16)
        oner = sbt("oner", [1, 128], BF16)
        onerf = sbt("onerf", [1, 128], F32)
        epsr = sbt("epsr", [1, 1], F32)
        bq_sb = sbt("bq_sb", [128, 48], F32)
        bo_sb = sbt("bo_sb", [128, DC], F32)
        bvr_sb = sbt("bvr_sb", [1, D], BF16)
        rows = sbt("rows", [1, 1536], F32)
        h64d = sbt("h64d", [128, 64], BF16)

        banks = [Bank(pst(f"pb{i}")) for i in range(8)]

        P = Prog()
        s_t = P.reg(sem("s_t"))
        s_v = P.reg(sem("s_v"))
        s_s = P.reg(sem("s_s"))
        s_cck = P.reg(sem("s_cck"))
        s_ccv = P.reg(sem("s_ccv"))
        s_rpk = P.reg(sem("s_rpk"))
        s_cst = P.reg(sem("s_cst"))
        s_cstg = P.reg(sem("s_cstg"))
        wr_ring = LoadRing(P, [P.reg(sem(f"swr{i}")) for i in range(WRN)])
        fr_ring = LoadRing(P, [P.reg(sem(f"sfr{i}")) for i in range(FRN)])
        m1_ring = LoadRing(P, [P.reg(sem(f"sm1{i}")) for i in range(4)])
        sgf_ring = StoreRing(P, [P.reg(sem(f"ssf{i}")) for i in range(SGFN)])
        pst_ring = StoreRing(P, [P.reg(sem(f"sps{i}")) for i in range(4)])
        sgb_ring = StoreRing(P, [P.reg(sem(f"ssb{i}")) for i in range(SGBN)])

        tile_store = {}
        tile_loads = {}

        def t_emit(fn, waits=(), last=False):
            return P.emit('T', fn, waits=waits, inc=(s_t if last else None))

        def barrier():
            tt = (s_t, P.cnt[s_t])
            tv = (s_v, P.cnt[s_v])
            ts = (s_s, P.cnt[s_s])
            P.emit('T', lambda e: e.nop(), waits=[tv, ts])
            P.emit('V', lambda e: e.nop(), waits=[tt, ts])
            P.emit('S', lambda e: e.nop(), waits=[tt, tv])

        def consts():
            P.emit('V', lambda e: e.memset(onec[:, :], 1.0), inc=s_v)
            P.emit('V', lambda e: e.memset(oner[:, :], 1.0), inc=s_v)
            P.emit('V', lambda e: e.memset(onerf[:, :], 1.0), inc=s_v)
            P.emit('V', lambda e: e.memset(epsr[:, :], EPS), inc=s_v)
            P.emit('Y', lambda e: e.dma_start(out=h128_sb[:, :], in_=h128[:, :]), inc=s_cst, n=16)
            P.emit('Y', lambda e: e.dma_start(out=h64_sb[:, :], in_=h64[:, :]), inc=s_cst, n=16)
            P.emit('Y', lambda e: e.dma_start(out=h64d[0:64, :], in_=h64[:, :]), inc=s_cst, n=16)
            P.emit('Y', lambda e: e.dma_start(out=h64d[64:128, :], in_=h64[:, :]), inc=s_cst, n=16)
            P.emit('Y', lambda e: e.dma_start(out=msk[:, 0:128], in_=masks[0, :, :]), inc=s_cst, n=16)
            P.emit('Y', lambda e: e.dma_start(out=id_sb[:, :], in_=ident[:, :]), inc=s_cst, n=16)
            P.emit('Y', lambda e: e.dma_start(out=msk[:, 128:256], in_=masks[1, :, :]), inc=s_cst, n=16)
            P.emit('Y', lambda e: e.dma_start(out=bq_sb[:, :], in_=bq[:, :]), inc=s_cst, n=16)
            P.emit('Y', lambda e: e.dma_start(out=bo_sb[:, :], in_=bo[:, :]), inc=s_cst, n=16)
            P.emit('G', lambda e: e.dma_start(out=bvr_sb[:, :], in_=bvr[:, :]), inc=s_cstg, n=16)
            return (s_cst, 9 * 16), (s_v, P.cnt[s_v])

        cst_tok, ones_tok = consts()

        def prologue_pass1(hcs):
            for hc in hcs:
                sl, idx, wtok = wr_ring.load(
                    'G', lambda s, hc=hc: (lambda e, s=s, hc=hc: e.dma_start(
                        out=wr[:, s * 2048:(s + 1) * 2048],
                        in_=w2t[hc * 128:(hc + 1) * 128, :])))
                ssl, stw, sttok = pst_ring.next()
                etoks = []
                for dq in range(4):
                    bk = banks[dq % 2]
                    tok = t_emit(
                        lambda e, s=sl, dq=dq, bk=bk: e.matmul(
                            bk.t[:, :], h128_sb[:, :],
                            wr[:, s * 2048 + dq * 512: s * 2048 + (dq + 1) * 512],
                            start=True, stop=True),
                        waits=[wtok, cst_tok, bk.rel], last=True)
                    if dq % 2 == 0:
                        etok = P.emit('S', lambda e, ssl=ssl, dq=dq, bk=bk: e.copy(
                            B[:, 24576 + ssl * 2048 + dq * 512: 24576 + ssl * 2048 + (dq + 1) * 512],
                            bk.t[:, :]),
                            waits=[tok, stw], inc=s_s)
                    else:
                        etok = P.emit('V', lambda e, ssl=ssl, dq=dq, bk=bk: e.tensor_copy(
                            B[:, 24576 + ssl * 2048 + dq * 512: 24576 + ssl * 2048 + (dq + 1) * 512],
                            bk.t[:, :]),
                            waits=[tok, stw], inc=s_v)
                    bk.rel = etok
                    etoks.append(etok)
                P.emit('S', lambda e, ssl=ssl, hc=hc: e.dma_start(
                    out=m1[hc, :, :], in_=B[:, 24576 + ssl * 2048: 24576 + (ssl + 1) * 2048]),
                    waits=etoks, inc=pst_ring.sems[ssl], n=16)
                wr_ring.release(idx, (s_t, P.cnt[s_t]))

        def prologue_pass2(pps, m1_done):
            for p in pps:
                sl, idx, wtok = m1_ring.load(
                    'Y', lambda s, half, p=p: (lambda e, s=s, half=half, p=p: e.dma_start(
                        out=B[half * 64:(half + 1) * 64, 16384 + s * D: 16384 + (s + 1) * D],
                        in_=m1[:, p + half, :])),
                    extra_waits=m1_done, ndma=2)
                ssl, stw, sttok = pst_ring.next()
                etoks = []
                for dq in range(4):
                    bk = banks[6 + dq % 2]
                    t_emit(lambda e, s=sl, dq=dq, bk=bk: e.matmul(
                        bk.t[0:64, :], h64d[0:64, :],
                        B[0:64, 16384 + s * D + dq * 512: 16384 + s * D + (dq + 1) * 512],
                        start=True, stop=True),
                        waits=[wtok, cst_tok, bk.rel], last=True)
                    tok = t_emit(lambda e, s=sl, dq=dq, bk=bk: e.matmul(
                        bk.t[64:128, :], h64d[64:128, :],
                        B[64:128, 16384 + s * D + dq * 512: 16384 + s * D + (dq + 1) * 512],
                        start=True, stop=True),
                        waits=[wtok], last=True)
                    if dq % 2 == 0:
                        etok = P.emit('S', lambda e, ssl=ssl, dq=dq, bk=bk: e.activation(
                            B[:, 24576 + ssl * 2048 + dq * 512: 24576 + ssl * 2048 + (dq + 1) * 512],
                            bk.t[:, :], AF.Identity, scale=WHS),
                            waits=[tok, stw], inc=s_s)
                    else:
                        etok = P.emit('V', lambda e, ssl=ssl, dq=dq, bk=bk: e.tensor_scalar_mul(
                            B[:, 24576 + ssl * 2048 + dq * 512: 24576 + ssl * 2048 + (dq + 1) * 512],
                            bk.t[:, :], WHS),
                            waits=[tok, stw], inc=s_v)
                    bk.rel = etok
                    etoks.append(etok)
                P.emit('S', lambda e, ssl=ssl, p=p: e.dma_start(
                    out=m2b[:, p:p + 2, :].rearrange("hc pp d -> pp hc d"),
                    in_=B[:, 24576 + ssl * 2048: 24576 + (ssl + 1) * 2048].rearrange(
                        "q (dq j) -> q dq j", j=512)),
                    waits=etoks, inc=pst_ring.sems[ssl], n=16)
                m1_ring.release(idx, (s_t, P.cnt[s_t]))

        s1_rel = [None] * S1N

        def ln_phase(src, dstbuf, dst_war=None):
            return [ln_tb(src, dstbuf, tb, dst_war) for tb in range(NTQ)]

        def ln_tb(src, dstbuf, tb, dst_war=None):
            if True:
                bsum, bsq = banks[4], banks[5]
                prev = []
                for ic in range(DC):
                    sl, idx, wtok = fr_ring.load(
                        'Y', lambda s, ic=ic, tb=tb, src=src: (
                            lambda e, s=s, ic=ic, tb=tb, src=src: e.dma_start(
                                out=fr[:, s * 512:(s + 1) * 512],
                                in_=src[ic, :, tb * 512:(tb + 1) * 512])),
                        extra_waits=[tile_store.get((src.name, ic, tb))])
                    tile_loads.setdefault((src.name, ic, tb), []).append(wtok)
                    xsl = ic % XBN
                    ctok = P.emit('S', lambda e, sl=sl, xsl=xsl: e.copy(
                        xb16[:, xsl * 512:(xsl + 1) * 512],
                        fr[:, sl * 512:(sl + 1) * 512]),
                        waits=[wtok, prev[ic - XBN][1] if ic >= XBN else None], inc=s_s)
                    t1 = t_emit(lambda e, xsl=xsl, ic=ic, bsum=bsum: e.matmul(
                        bsum.t[0:1, :], onec[:, :],
                        xb16[:, xsl * 512:(xsl + 1) * 512],
                        start=(ic == 0), stop=(ic == DC - 1)),
                        waits=[ctok, ones_tok, bsum.rel if ic == 0 else None],
                        last=True)
                    qtok = P.emit('S', lambda e, sl=sl, xsl=xsl: e.activation(
                        xb16[:, xsl * 512:(xsl + 1) * 512],
                        fr[:, sl * 512:(sl + 1) * 512], AF.Square),
                        waits=[t1], inc=s_s)
                    t2 = t_emit(lambda e, xsl=xsl, ic=ic, bsq=bsq: e.matmul(
                        bsq.t[0:1, :], onec[:, :],
                        xb16[:, xsl * 512:(xsl + 1) * 512],
                        start=(ic == 0), stop=(ic == DC - 1)),
                        waits=[qtok, bsq.rel if ic == 0 else None],
                        last=True)
                    fr_ring.release(idx, qtok)
                    prev.append((t1, t2))
                sum_tok = prev[DC - 1][0]
                sq_tok = prev[DC - 1][1]
                r_mean = rows[0:1, 0:512]
                r_tmp = rows[0:1, 512:1024]
                r_rstd = rows[0:1, 1024:1536]
                r_nmr = rows[0:1, 512:1024]
                tok = P.emit('S', lambda e, bsum=bsum: e.mul(r_mean, bsum.t[0:1, :], 1.0 / D),
                             waits=[sum_tok], inc=s_s)
                bsum.rel = tok
                tok2 = P.emit('S', lambda e: e.activation(r_tmp, r_mean, AF.Square),
                              waits=[tok], inc=s_s)
                tok3 = P.emit('V', lambda e, bsq=bsq: e.scalar_tensor_tensor(
                    r_tmp, bsq.t[0:1, :], 1.0 / D, r_tmp, ALU.mult, ALU.subtract),
                    waits=[tok2, sq_tok], inc=s_v)
                bsq.rel = tok3
                tok4 = P.emit('S', lambda e: e.activation(r_tmp, r_tmp, AF.Ln, bias=epsr[:, :]),
                              waits=[tok3, cst_tok], inc=s_s)
                tok5 = P.emit('S', lambda e: e.activation(r_rstd, r_tmp, AF.Exp, scale=-0.5),
                              waits=[tok4], inc=s_s)
                tok6 = P.emit('V', lambda e: e.tensor_mul(r_nmr, r_mean, r_rstd),
                              waits=[tok5], inc=s_v)
                tok7 = P.emit('S', lambda e: e.mul(r_nmr, r_nmr, -1.0), waits=[tok6], inc=s_s)
                bbc = banks[6]
                t1b = t_emit(lambda e, bbc=bbc: e.matmul(
                    bbc.t[:, :], onerf[:, :], r_rstd, start=True, stop=True),
                    waits=[tok5, ones_tok, bbc.rel], last=True)
                e1 = P.emit('V', lambda e, bbc=bbc: e.tensor_copy(lnb[:, 0:512], bbc.t[:, :]),
                            waits=[t1b], inc=s_v)
                bbc.rel = e1
                t2b = t_emit(lambda e, bbc=bbc: e.matmul(
                    bbc.t[:, :], onerf[:, :], r_nmr, start=True, stop=True),
                    waits=[tok7, bbc.rel], last=True)
                e2 = P.emit('V', lambda e, bbc=bbc: e.tensor_copy(lnb[:, 512:1024], bbc.t[:, :]),
                            waits=[t2b], inc=s_v)
                bbc.rel = e2
                for ic in range(DC):
                    sl, idx, wtok = fr_ring.load(
                        'Y', lambda s, ic=ic, tb=tb, src=src: (
                            lambda e, s=s, ic=ic, tb=tb, src=src: e.dma_start(
                                out=fr[:, s * 512:(s + 1) * 512],
                                in_=src[ic, :, tb * 512:(tb + 1) * 512])),
                        extra_waits=[tile_store.get((src.name, ic, tb))])
                    tile_loads.setdefault((src.name, ic, tb), []).append(wtok)
                    ssl = sl % S1N
                    mtok = P.emit('V', lambda e, sl=sl, ssl=ssl: e.tensor_mul(
                        s1[:, ssl * 512:(ssl + 1) * 512],
                        fr[:, sl * 512:(sl + 1) * 512], lnb[:, 0:512]),
                        waits=[wtok, e1, s1_rel[ssl]], inc=s_v)
                    atok = P.emit('V', lambda e, sl=sl, ssl=ssl, ic=ic, tb=tb, dstbuf=dstbuf: e.tensor_add(
                        dstbuf[:, ic * T + tb * 512: ic * T + (tb + 1) * 512],
                        s1[:, ssl * 512:(ssl + 1) * 512], lnb[:, 512:1024]),
                        waits=[mtok, e2, dst_war], inc=s_v)
                    s1_rel[ssl] = atok
                    fr_ring.release(idx, atok)
            return atok

        def ffn_run(lnbuf, gbuf, src0, dst, w2ht_ready_fn, tb_toks=None, gwar0=None, aux=None):
            state = {}
            state[('gwar', 0)] = gwar0
            hu = [0]
            yu = [0]
            for og in range(NOG):
                g_tok_last = None
                for oc16 in range(16):
                    oc = og * 16 + oc16
                    sl1, idx1, wtok1 = wr_ring.load(
                        'G', lambda s, oc=oc: (lambda e, s=s, oc=oc: e.dma_start(
                            out=wr[:, s * 2048:(s + 1) * 2048], in_=w1s[oc, :, :])))
                    sl3, idx3, wtok3 = wr_ring.load(
                        'G', lambda s, oc=oc: (lambda e, s=s, oc=oc: e.dma_start(
                            out=wr[:, s * 2048:(s + 1) * 2048], in_=w3s[oc, :, :])))
                    for tq in range(NTQ):
                        u = hu[0]
                        bA = banks[(u % 3) * 2]
                        bH = banks[(u % 3) * 2 + 1]
                        tokh1 = None
                        for ic in range(ICN):
                            tokh1 = t_emit(
                                lambda e, sl=sl1, ic=ic, tq=tq, bk=bA, lnbuf=lnbuf,
                                st=(ic == 0), sp=(ic == ICN - 1): e.matmul(
                                    bk.t[:, :],
                                    wr[:, sl * 2048 + ic * 128: sl * 2048 + (ic + 1) * 128],
                                    lnbuf[:, ic * T + tq * 512: ic * T + (tq + 1) * 512],
                                    start=st, stop=sp),
                                waits=[wtok1, bA.rel if ic == 0 else None,
                                       tb_toks[tq] if (ic == 0 and tb_toks) else None],
                                last=(ic == ICN - 1))
                        tokh3 = None
                        for ic in range(ICN):
                            tokh3 = t_emit(
                                lambda e, sl=sl3, ic=ic, tq=tq, bk=bH, lnbuf=lnbuf,
                                st=(ic == 0), sp=(ic == ICN - 1): e.matmul(
                                    bk.t[:, :],
                                    wr[:, sl * 2048 + ic * 128: sl * 2048 + (ic + 1) * 128],
                                    lnbuf[:, ic * T + tq * 512: ic * T + (tq + 1) * 512],
                                    start=st, stop=sp),
                                waits=[wtok3, bH.rel if ic == 0 else None],
                                last=(ic == ICN - 1))
                        stok = P.emit('S', lambda e, u=u, bk=bA: e.activation(
                            s1[:, (u % S1N) * 512:(u % S1N + 1) * 512], bk.t[:, :],
                            AF.Silu),
                            waits=[tokh1, s1_rel[u % S1N]], inc=s_s)
                        bA.rel = stok
                        gtok = P.emit('V', lambda e, u=u, oc16=oc16, tq=tq, bk=bH, gbuf=gbuf: e.tensor_mul(
                            gbuf[:, oc16 * T + tq * 512: oc16 * T + (tq + 1) * 512],
                            s1[:, (u % S1N) * 512:(u % S1N + 1) * 512], bk.t[:, :]),
                            waits=[stok, tokh3, state.get(('gwar', og)),
                                   state.get('prol_final') if (og == 0 and oc16 >= 8) else None],
                            inc=s_v)
                        bH.rel = gtok
                        s1_rel[u % S1N] = gtok
                        g_tok_last = gtok
                        hu[0] += 1
                    wr_ring.release(idx1, (s_t, P.cnt[s_t]))
                    wr_ring.release(idx3, (s_t, P.cnt[s_t]))
                    if aux is not None and og == 0 and oc16 < 8:
                        aux(oc16)
                        if oc16 == 7:
                            state['prol_final'] = (
                                [(sm, P.cnt[sm]) for sm in pst_ring.sems]
                                + [(sm, P.cnt[sm]) for sm in m1_ring.sems])
                if og == NOG - 1:
                    state['last_h'] = (s_t, P.cnt[s_t])
                for dc in range(DC):
                    def _w2load(s, q4, og=og, dc=dc):
                        def go(e, s=s, q4=q4, og=og, dc=dc):
                            return e.dma_start(
                                out=wr[:, s * 2048 + q4 * 512:
                                       s * 2048 + (q4 + 1) * 512].rearrange(
                                    "p (oc j) -> p oc j", j=128),
                                in_=m2b[og * 16 + q4 * 4:og * 16 + (q4 + 1) * 4,
                                        :, dc * 128:(dc + 1) * 128].rearrange(
                                    "oc p j -> p oc j"))
                        return go
                    slw, idxw, wtokw = wr_ring.load('G', _w2load,
                                                    extra_waits=w2ht_ready_fn(),
                                                    ndma=4)
                    for tq in range(NTQ):
                        bk = banks[6 + yu[0] % 2]
                        tok = None
                        for k in range(16):
                            tok = t_emit(
                                lambda e, sl=slw, k=k, tq=tq, bk=bk, gbuf=gbuf,
                                st=(k == 0), sp=(k == 15): e.matmul(
                                    bk.t[:, :],
                                    wr[:, sl * 2048 + k * 128: sl * 2048 + (k + 1) * 128],
                                    gbuf[:, k * T + tq * 512: k * T + (tq + 1) * 512],
                                    start=st, stop=sp),
                                waits=[wtokw, g_tok_last if k == 0 else None,
                                       bk.rel if k == 0 else None],
                                last=(k == 15))
                        rsrc = src0 if og == 0 else stream
                        rdst = dst if og == NOG - 1 else stream
                        slr, idxr, rtok = fr_ring.load(
                            'Y', lambda s, rsrc=rsrc, dc=dc, tq=tq: (
                                lambda e, s=s, rsrc=rsrc, dc=dc, tq=tq: e.dma_start(
                                    out=fr[:, s * 512:(s + 1) * 512],
                                    in_=rsrc[dc, :, tq * 512:(tq + 1) * 512])),
                            extra_waits=[tile_store.get((rsrc.name, dc, tq))])
                        tile_loads.setdefault((rsrc.name, dc, tq), []).append(rtok)
                        ssl, stw, sttok = sgf_ring.next()
                        etok = P.emit('V', lambda e, ssl=ssl, slr=slr, bk=bk: e.scalar_tensor_tensor(
                            sgf[:, ssl * 512:(ssl + 1) * 512], bk.t[:, :], 0.5,
                            fr[:, slr * 512:(slr + 1) * 512], ALU.mult, ALU.add),
                            waits=[tok, rtok, stw], inc=s_v)
                        bk.rel = etok
                        fr_ring.release(idxr, etok)
                        ldtoks = tile_loads.pop((rdst.name, dc, tq), [])
                        P.emit('S', lambda e, ssl=ssl, rdst=rdst, dc=dc, tq=tq: e.dma_start(
                            out=rdst[dc, :, tq * 512:(tq + 1) * 512],
                            in_=sgf[:, ssl * 512:(ssl + 1) * 512]),
                            waits=[etok] + ldtoks, inc=sgf_ring.sems[ssl], n=16)
                        tile_store[(rdst.name, dc, tq)] = sttok
                        yu[0] += 1
                    wr_ring.release(idxw, (s_t, P.cnt[s_t]))
                state[('gwar', og + 1)] = (s_t, P.cnt[s_t])
                if og == NOG - 1:
                    pass
                if og == NOG - 2:
                    pass
            state['last_y'] = state[('gwar', NOG)]
            return state

        def qkv(lnbuf, qbuf, q_war, tb_toks=None):
            for h in range(NH):
                oc = 16 + h
                slw, idxw, wtokw = wr_ring.load(
                    'G', lambda s, oc=oc: (lambda e, s=s, oc=oc: e.dma_start(
                        out=wr[:, s * 2048:(s + 1) * 2048], in_=wqs[oc, :, :])))
                for tq in range(NTQ):
                    bk = banks[(h * NTQ + tq) % 6]
                    tok = None
                    for ic in range(ICN):
                        tok = t_emit(
                            lambda e, sl=slw, ic=ic, tq=tq, bk=bk, lnbuf=lnbuf,
                            st=(ic == 0), sp=(ic == ICN - 1): e.matmul(
                                bk.t[:, :],
                                wr[:, sl * 2048 + ic * 128: sl * 2048 + (ic + 1) * 128],
                                lnbuf[:, ic * T + tq * 512: ic * T + (tq + 1) * 512],
                                start=st, stop=sp),
                            waits=[wtokw, bk.rel if ic == 0 else None,
                                   tb_toks[tq] if (ic == 0 and tb_toks) else None],
                            last=(ic == ICN - 1))
                    ssl, stw, sttok = sgb_ring.next()
                    etok = P.emit('S', lambda e, ssl=ssl, bk=bk, oc=oc: e.activation(
                        sgb[:, ssl * 512:(ssl + 1) * 512], bk.t[:, :], AF.Identity,
                        bias=bq_sb[:, oc:oc + 1]),
                        waits=[tok, stw, cst_tok], inc=s_s)
                    bk.rel = etok
                    P.emit('S', lambda e, ssl=ssl, h=h, tq=tq: e.dma_start(
                        out=k_own[h, :, tq * 512:(tq + 1) * 512],
                        in_=sgb[:, ssl * 512:(ssl + 1) * 512]),
                        waits=[etok], inc=sgb_ring.sems[ssl], n=16)
                wr_ring.release(idxw, (s_t, P.cnt[s_t]))
            cck = P.emit('G', lambda e: e.collective_compute(
                "AllGather", ALU.bypass,
                replica_groups=[[0, 1], [2, 3], [4, 5], [6, 7]],
                ins=[k_own.ap().opt()], outs=[k_ag.ap().opt()]),
                waits=[(sm, P.cnt[sm]) for sm in sgb_ring.sems], inc=s_cck)
            for g4 in range(4):
                slots4 = []
                for q in range(4):
                    slw, idxw, wtokw = wr_ring.load(
                        'G', lambda s, g4=g4, q=q: (lambda e, s=s, g4=g4, q=q: e.dma_start(
                            out=wr[:, s * 2048:(s + 1) * 2048],
                            in_=wvs4[g4, :, q * 2048:(q + 1) * 2048])))
                    slots4.append((slw, idxw, wtokw))
                vwt = [sl[2] for sl in slots4]
                for tt in range(QT):
                    bk = banks[6 + tt % 2]
                    tok = None
                    for ic in range(ICN):
                        vsl = slots4[ic // 4][0]
                        tok = t_emit(
                            lambda e, sl=vsl, ic=ic, tt=tt, bk=bk, lnbuf=lnbuf,
                            st=(ic == 0): e.matmul(
                                bk.t[:, :],
                                lnbuf[:, ic * T + tt * 128: ic * T + (tt + 1) * 128],
                                wr[:, sl * 2048 + (ic % 4) * 512: sl * 2048 + (ic % 4 + 1) * 512],
                                start=st, stop=False),
                            waits=[vwt[ic // 4], bk.rel if ic == 0 else None,
                                   tb_toks[tt // 4] if (ic == 0 and tb_toks) else None])
                    tok = t_emit(lambda e, bk=bk, g4=g4: e.matmul(
                        bk.t[:, :], oner[:, :],
                        bvr_sb[:, g4 * 512:(g4 + 1) * 512],
                        start=False, stop=True),
                        waits=[ones_tok, (s_cstg, 16)], last=True)
                    ssl, stw, sttok = sgb_ring.next()
                    etok = P.emit('S', lambda e, ssl=ssl, bk=bk: e.copy(
                        sgb[:, ssl * 512:(ssl + 1) * 512], bk.t[:, :]),
                        waits=[tok, stw], inc=s_s)
                    bk.rel = etok
                    P.emit('S', lambda e, ssl=ssl, g4=g4, tt=tt: e.dma_start(
                        out=v_own[g4 * 4:(g4 + 1) * 4, tt, :, :].rearrange(
                            "he p ee -> p he ee"),
                        in_=sgb[:, ssl * 512:(ssl + 1) * 512].rearrange(
                            "p (he ee) -> p he ee", ee=128)),
                        waits=[etok], inc=sgb_ring.sems[ssl], n=16)
                for (slw, idxw, _) in slots4:
                    wr_ring.release(idxw, (s_t, P.cnt[s_t]))
            ccv = P.emit('G', lambda e: e.collective_compute(
                "AllGather", ALU.bypass,
                replica_groups=[[0, 1], [2, 3], [4, 5], [6, 7]],
                ins=[v_own.ap().opt()], outs=[v_ag.ap().opt()]),
                waits=[(sm, P.cnt[sm]) for sm in sgb_ring.sems], inc=s_ccv)
            for h in range(NH):
                oc = h
                slw, idxw, wtokw = wr_ring.load(
                    'G', lambda s, oc=oc: (lambda e, s=s, oc=oc: e.dma_start(
                        out=wr[:, s * 2048:(s + 1) * 2048], in_=wqs[oc, :, :])))
                for tq in range(NTQ):
                    bk = banks[(h * NTQ + tq) % 6]
                    tok = None
                    for ic in range(ICN):
                        tok = t_emit(
                            lambda e, sl=slw, ic=ic, tq=tq, bk=bk, lnbuf=lnbuf,
                            st=(ic == 0), sp=(ic == ICN - 1): e.matmul(
                                bk.t[:, :],
                                wr[:, sl * 2048 + ic * 128: sl * 2048 + (ic + 1) * 128],
                                lnbuf[:, ic * T + tq * 512: ic * T + (tq + 1) * 512],
                                start=st, stop=sp),
                            waits=[wtokw, bk.rel if ic == 0 else None,
                                   tb_toks[tq] if (ic == 0 and tb_toks) else None],
                            last=(ic == ICN - 1))
                    etok = P.emit('S', lambda e, bk=bk, h=h, tq=tq, oc=oc, qbuf=qbuf: e.activation(
                        qbuf[:, h * T + tq * 512: h * T + (tq + 1) * 512],
                        bk.t[:, :], AF.Identity, bias=bq_sb[:, oc:oc + 1]),
                        waits=[tok, cst_tok, q_war], inc=s_s)
                    bk.rel = etok
                wr_ring.release(idxw, (s_t, P.cnt[s_t]))
            return (s_cck, P.cnt[s_cck]), (s_ccv, P.cnt[s_ccv]), (s_t, P.cnt[s_t])

        def attention(qbuf, obuf, cck, ccv, o_war):
            q_ready = (s_s, P.cnt[s_s])
            pt_rel = [None] * PTN
            rcb_rel = [None] * RBN
            uu = [0]
            o_last = None
            div_pending = []

            def div_flush():
                bl, bO, ltok, ottok, h, g = div_pending.pop(0)
                r_l = rows[0:1, 0:512]
                r_rc = rows[0:1, 512:1024]
                t1 = P.emit('S', lambda e, bl=bl: e.activation(
                    r_l, bl.t[0:1, :], AF.Ln), waits=[ltok], inc=s_s)
                bl.rel = t1
                t2 = P.emit('S', lambda e: e.activation(r_rc, r_l, AF.Exp, scale=-1.0),
                            waits=[t1], inc=s_s)
                bbc = banks[6]
                t3 = t_emit(lambda e, bbc=bbc: e.matmul(
                    bbc.t[:, :], onerf[:, :], r_rc, start=True, stop=True),
                    waits=[t2, ones_tok, bbc.rel], last=True)
                gg = (h * 4 + g) % RBN
                t4 = P.emit('V', lambda e, bbc=bbc, gg=gg: e.tensor_copy(
                    rcb[:, gg * 512:(gg + 1) * 512], bbc.t[:, :]),
                    waits=[t3, rcb_rel[gg]], inc=s_v)
                bbc.rel = t4
                t5 = P.emit('V', lambda e, bk=bO, gg=gg, h=h, g=g, obuf=obuf: e.tensor_mul(
                    obuf[:, h * T + g * 512: h * T + (g + 1) * 512],
                    bk.t[:, :], rcb[:, gg * 512:(gg + 1) * 512]),
                    waits=[t4, ottok, o_war], inc=s_v)
                bO.rel = t5
                rcb_rel[gg] = t5
                return t5
            for h in range(NH):
                slots = []
                for r in range(2):
                    slk, idxk, wtk = wr_ring.load(
                        'G', lambda s, r=r, h=h: (lambda e, s=s, r=r, h=h: e.dma_start(
                            out=wr[:, s * 2048:(s + 1) * 2048],
                            in_=k_ag[r * NH + h, :, :])),
                        extra_waits=[cck])
                    slots.append((slk, idxk, wtk))
                for r in range(2):
                    def _vload(s, q4, r=r, h=h):
                        def go(e, s=s, q4=q4, r=r, h=h):
                            return e.dma_start(
                                out=wr[:, s * 2048 + q4 * 512:
                                       s * 2048 + (q4 + 1) * 512].rearrange(
                                    "p (tt ee) -> p tt ee", ee=128),
                                in_=v_ag[r * NH + h, q4 * 4:(q4 + 1) * 4,
                                         :, :].rearrange("tt p ee -> p tt ee"))
                        return go
                    slv, idxv, wtv = wr_ring.load('G', _vload,
                                                  extra_waits=[ccv], ndma=4)
                    slots.append((slv, idxv, wtv))
                ktoks = [slots[0][2], slots[1][2]]
                vtoks = [slots[2][2], slots[3][2]]
                for g in range(4):
                    if div_pending:
                        o_last = div_flush()
                    bl = banks[4 + g % 2]
                    bO = banks[2 + g % 2]
                    jn = 8 * g + 8
                    ltok = None
                    ottok = None
                    pending = []

                    def flush_one(g=g, jn=jn, bl=bl, bO=bO):
                        nonlocal ltok, ottok
                        j, u, off, ptok = pending.pop(0)
                        ltok = t_emit(lambda e, off=off, u=u, bl=bl,
                                      st=(j == 0), sp=(j == jn - 1): e.matmul(
                            bl.t[0:1, off:512], onec[:, :],
                            pt[:, (u % PTN) * 512 + off:(u % PTN) * 512 + 512],
                            start=st, stop=sp),
                            waits=[ptok, ones_tok, bl.rel if j == 0 else None],
                            last=(j == jn - 1))
                        vap = wr[:, slots[2 + j % 2][0] * 2048 + (j // 2) * 128:
                                 slots[2 + j % 2][0] * 2048 + (j // 2 + 1) * 128]
                        ottok = t_emit(lambda e, vap=vap, off=off, u=u, bk=bO,
                                       st=(j == 0), sp=(j == jn - 1): e.matmul(
                            bk.t[:, off:512], vap,
                            pt[:, (u % PTN) * 512 + off:(u % PTN) * 512 + 512],
                            start=st, stop=sp),
                            waits=[vtoks[j % 2], bO.rel if j == 0 else None],
                            last=(j == jn - 1))
                        pt_rel[u % PTN] = ottok

                    for j in range(jn):
                        off = max(0, j // 2 - 4 * g) * 128
                        u = uu[0]
                        bs = banks[u % 2]
                        kap = wr[:, slots[j % 2][0] * 2048 + (j // 2) * 128:
                                 slots[j % 2][0] * 2048 + (j // 2 + 1) * 128]
                        has_mask = (j >= 8 * g)
                        stok = t_emit(lambda e, kap=kap, g=g, off=off, bs=bs, h=h, qbuf=qbuf,
                                      sp=(not has_mask): e.matmul(
                            bs.t[:, off:512], kap,
                            qbuf[:, h * T + g * 512 + off: h * T + (g + 1) * 512],
                            start=True, stop=sp),
                            waits=[ktoks[j % 2], q_ready, bs.rel], last=(not has_mask))
                        mwait = stok
                        if has_mask:
                            mc = 0 if j % 2 == 0 else 128
                            mwait = t_emit(lambda e, bs=bs, off=off, mc=mc: e.matmul(
                                bs.t[:, off:off + 128], id_sb[:, :],
                                msk[:, mc:mc + 128], start=False, stop=True),
                                waits=[cst_tok], last=True)
                        ptok = P.emit('S', lambda e, bs=bs, off=off, u=u: e.activation(
                            pt[:, (u % PTN) * 512 + off:(u % PTN) * 512 + 512],
                            bs.t[:, off:512], AF.Exp, scale=SQS),
                            waits=[mwait, pt_rel[u % PTN]], inc=s_s)
                        bs.rel = ptok
                        pending.append((j, u, off, ptok))
                        if len(pending) > 2:
                            flush_one()
                        uu[0] += 1
                    while pending:
                        flush_one()
                    div_pending.append((bl, bO, ltok, ottok, h, g))
                for (sl, idx, _) in slots:
                    wr_ring.release(idx, (s_t, P.cnt[s_t]))
            while div_pending:
                o_last = div_flush()
            return o_last, (s_t, P.cnt[s_t])

        def wout(obuf, o_ready):
            for dc in range(DC):
                slw, idxw, wtokw = wr_ring.load(
                    'G', lambda s, dc=dc: (lambda e, s=s, dc=dc: e.dma_start(
                        out=wr[:, s * 2048:(s + 1) * 2048], in_=wos[dc, :, :])))
                for tq in range(NTQ):
                    bk = banks[(dc * NTQ + tq) % 6]
                    tok = None
                    for hc in range(NH):
                        tok = t_emit(
                            lambda e, sl=slw, hc=hc, tq=tq, bk=bk, obuf=obuf,
                            st=(hc == 0), sp=(hc == NH - 1): e.matmul(
                                bk.t[:, :],
                                wr[:, sl * 2048 + hc * 128: sl * 2048 + (hc + 1) * 128],
                                obuf[:, hc * T + tq * 512: hc * T + (tq + 1) * 512],
                                start=st, stop=sp),
                            waits=[wtokw, o_ready if hc == 0 else None,
                                   bk.rel if hc == 0 else None],
                            last=(hc == NH - 1))
                    slr, idxr, rtok = fr_ring.load(
                        'Y', lambda s, dc=dc, tq=tq: (
                            lambda e, s=s, dc=dc, tq=tq: e.dma_start(
                                out=fr[:, s * 512:(s + 1) * 512],
                                in_=stream[dc, :, tq * 512:(tq + 1) * 512])),
                        extra_waits=[tile_store.get((stream.name, dc, tq))])
                    tile_loads.setdefault((stream.name, dc, tq), []).append(rtok)
                    ssl, stw, sttok = sgf_ring.next()
                    etok = P.emit('V', lambda e, ssl=ssl, slr=slr, bk=bk, dc=dc: e.scalar_tensor_tensor(
                        sgf[:, ssl * 512:(ssl + 1) * 512], bk.t[:, :],
                        bo_sb[:, dc:dc + 1],
                        fr[:, slr * 512:(slr + 1) * 512], ALU.add, ALU.add),
                        waits=[tok, rtok, stw, cst_tok], inc=s_v)
                    bk.rel = etok
                    fr_ring.release(idxr, etok)
                    ldtoks = tile_loads.pop((stream.name, dc, tq), [])
                    P.emit('S', lambda e, ssl=ssl, dc=dc, tq=tq: e.dma_start(
                        out=stream[dc, :, tq * 512:(tq + 1) * 512],
                        in_=sgf[:, ssl * 512:(ssl + 1) * 512]),
                        waits=[etok] + ldtoks, inc=sgf_ring.sems[ssl], n=16)
                    tile_store[(stream.name, dc, tq)] = sttok
                wr_ring.release(idxw, (s_t, P.cnt[s_t]))
            return (s_t, P.cnt[s_t])

        tb1 = []
        for i in range(NTQ):
            prologue_pass1(range(i * 16, (i + 1) * 16))
            tb1.append(ln_tb(xin, A, i))
        m1_done = [(sm, P.cnt[sm]) for sm in pst_ring.sems]

        def _pass2_aux(oc16, m1_done=m1_done):
            prologue_pass2(range(oc16 * 16, (oc16 + 1) * 16, 2), m1_done)

        def _w2rdy():
            return [(sm, P.cnt[sm]) for sm in pst_ring.sems]

        st1 = ffn_run(A, B, xin, stream, _w2rdy, tb_toks=tb1, aux=_pass2_aux)
        tb2 = ln_phase(stream, A, dst_war=st1['last_h'])
        cck, ccv, qkv_last = qkv(A, B, st1['last_y'], tb_toks=tb2)
        o_ready, attn_last = attention(B, A, cck, ccv, o_war=qkv_last)
        wout_last = wout(A, o_ready)
        tb3 = ln_phase(stream, B, dst_war=attn_last)
        ffn_run(B, A, stream, xout, lambda: [], tb_toks=tb3, gwar0=wout_last)

        fin = [(sm, P.cnt[sm]) for sm in sgf_ring.sems]
        P.emit('Y', lambda e: e.nop(), waits=fin)

        with nc.Block() as block:
            def mk(key):
                def body(eng):
                    seen = {}
                    for (fn, ws, inc, n) in P.q[key]:
                        for (s, v) in ws:
                            if seen.get(s, 0) < v:
                                eng.wait_ge(s, v)
                                seen[s] = v
                        inst = fn(eng)
                        if inc is not None:
                            inst.then_inc(inc, n)
                return body
            block.tensor(mk('T'))
            block.vector(mk('V'))
            block.scalar(mk('S'))
            block.sync(mk('Y'))
            block.gpsimd(mk('G'))

    return nc


_CACHED = {}


def _host_prep(L0_attn_x, L0_attn_W_qkv, L0_attn_b_qkv, L0_attn_W_out,
               L0_attn_b_out, L0_ffn_W1, L0_ffn_W2, L0_ffn_W3):
    x = np.asarray(L0_attn_x, dtype=np.float32)
    W1 = np.asarray(L0_ffn_W1, dtype=np.float32)
    W2 = np.asarray(L0_ffn_W2, dtype=np.float32)
    W3 = np.asarray(L0_ffn_W3, dtype=np.float32)
    Wqkv = np.asarray(L0_attn_W_qkv, dtype=np.float32)
    bqkv = np.asarray(L0_attn_b_qkv, dtype=np.float32)
    Wout = np.asarray(L0_attn_W_out, dtype=np.float32)
    bout = np.asarray(L0_attn_b_out, dtype=np.float32)

    def tile4(w, noc):
        return np.ascontiguousarray(
            w.reshape(noc, 128, 16, 128).transpose(0, 3, 2, 1).reshape(noc, 128, 2048))

    w1s = tile4(W1, 64)
    w3s = tile4(W3, 64)
    wqs = tile4(Wqkv, 48)
    wv = Wqkv[4096:6144]
    wvs4 = np.ascontiguousarray(
        wv.reshape(4, 4, 128, 16, 128).transpose(0, 4, 3, 1, 2).reshape(4, 128, 8192))
    wos = tile4(Wout, 16)
    w2t = np.ascontiguousarray(W2.T)
    bqh = np.ascontiguousarray(bqkv.reshape(48, 128).T)
    bvr = np.ascontiguousarray(bqkv[4096:6144].reshape(1, 2048))
    boh = np.ascontiguousarray(bout.reshape(16, 128).T)
    h128a = np.ascontiguousarray(_hadamard(128)).astype(ml_dtypes.bfloat16)
    h64a = np.ascontiguousarray(_hadamard(64)).astype(ml_dtypes.bfloat16)
    identa = np.eye(128, dtype=np.float32).astype(ml_dtypes.bfloat16)

    diag_mask = np.where(np.arange(128)[:, None] <= np.arange(128)[None, :],
                         0.0, NEGBIG).astype(np.float32)
    full_mask = np.full((128, 128), NEGBIG, dtype=np.float32)
    zero_mask = np.zeros((128, 128), dtype=np.float32)
    masks_by_parity = [
        np.stack([diag_mask, full_mask]),
        np.stack([zero_mask, diag_mask]),
    ]

    in_maps = []
    rows_by_core = []
    for c in range(8):
        b, par = divmod(c, 2)
        tiles = [2 * i + par for i in range(16)]
        rws = np.concatenate([np.arange(g * 128, (g + 1) * 128) for g in tiles])
        rows_by_core.append((b, rws))
        xc = np.ascontiguousarray(x[b].reshape(4096, 2048)[rws].T).reshape(16, 128, 2048)
        in_maps.append({
            "xin": xc, "w1s": w1s, "w3s": w3s, "w2t": w2t, "wqs": wqs,
            "wos": wos, "wvs4": wvs4, "bq": bqh, "bvr": bvr, "bo": boh,
            "h128": h128a, "h64": h64a, "ident": identa,
            "masks": np.ascontiguousarray(masks_by_parity[par]).astype(ml_dtypes.bfloat16),
        })
    return in_maps, rows_by_core


def kernel(L0_attn_x, L0_attn_W_qkv, L0_attn_b_qkv, L0_attn_W_out,
           L0_attn_b_out, L0_ffn_W1, L0_ffn_W2, L0_ffn_W3):
    in_maps, rows_by_core = _host_prep(
        L0_attn_x, L0_attn_W_qkv, L0_attn_b_qkv, L0_attn_W_out,
        L0_attn_b_out, L0_ffn_W1, L0_ffn_W2, L0_ffn_W3)
    if 'nc' not in _CACHED:
        _CACHED['nc'] = build_program()
    nc = _CACHED['nc']
    res = run_bass_kernel_spmd(nc, in_maps, core_ids=list(range(8)))
    out = np.empty((4, 4096, 2048), dtype=np.float32)
    for c in range(8):
        b, rws = rows_by_core[c]
        xo = np.asarray(res.results[c]["xout"]).reshape(2048, 2048)
        out[b].reshape(4096, 2048)[rws] = np.ascontiguousarray(xo.T)
    return out
